# version 13
# baseline (speedup 1.0000x reference)
import math
import sys
from contextlib import ExitStack

import numpy as np

try:
    import concourse.bass as bass
except ImportError:
    sys.path.insert(0, '/opt/trn_rl_repo')
    import concourse.bass as bass
import concourse.mybir as mybir
from concourse.bass_utils import run_bass_kernel_spmd

F32 = mybir.dt.float32
BF16 = mybir.dt.bfloat16
FP16 = mybir.dt.float16
FP8 = mybir.dt.float8e4
ALU = mybir.AluOpType
AF = mybir.ActivationFunctionType
AX = mybir.AxisListType

B, T, D = 2, 8192, 1024
SEG, NBLK = 128, 64
DILS = [1, 2, 4, 8]
NORMC = 1.0 / math.sqrt(128.0)
EPS = 1e-5
NT = 16
NTOK = 2048
P = 128
NEG = -1.0e30
L_H = [64, 32, 16, 8]
G_H = [16, 8, 4, 2]
ATT_OFF = [0, 2048, 3072, 3584]
ATT_COLS = 3840


def _pi_perm():
    offs = [np.arange(0, 64, 8), np.arange(4, 64, 8),
            np.arange(2, 64, 4), np.arange(1, 64, 2)]
    return np.concatenate(
        [(np.arange(32)[:, None] * 64 + o[None, :]).ravel() for o in offs])


def _masks():
    m = np.empty((4, P, P), np.float32)
    for h, L in enumerate(L_H):
        idx = np.arange(P)
        ph, n = idx // L, idx % L
        ok = (ph[:, None] == ph[None, :]) & (n[:, None] >= n[None, :])
        m[h] = np.where(ok, 0.0, NEG)
    return m


def _inc_if(inst, cond, sem):
    if cond:
        inst.then_inc(sem, 1)
    return inst


class _Sem:
    def __init__(self, handle):
        self.h = handle
        self.n = 0

    def inc(self, k=1):
        self.n += k
        return self.n


def _build():
    nc = bass.Bass('TRN2', target_bir_lowering=False)
    dt_in = dict(kind='ExternalInput')
    xa_d = nc.dram_tensor('xa', [NTOK, D], F32, **dt_in).ap()
    xf_d = nc.dram_tensor('xf', [NTOK, D], F32, **dt_in).ap()
    xffm_d = nc.dram_tensor('xffm', [D, NTOK], F32, **dt_in).ap()
    wqkv_d = nc.dram_tensor('wqkv', [4, 3, P, 1024], F32, **dt_in).ap()
    wo_d = nc.dram_tensor('wo', [4, P, 1024], F32, **dt_in).ap()
    wf_d = nc.dram_tensor('wf', [8, P, 1024], F32, **dt_in).ap()
    bo_d = nc.dram_tensor('bo', [P, 8], F32, **dt_in).ap()
    bfn_d = nc.dram_tensor('bfn', [P, 8], F32, **dt_in).ap()
    msk_d = nc.dram_tensor('msk', [4, P, P], F32, **dt_in).ap()
    idn_d = nc.dram_tensor('idn', [P, P], F32, **dt_in).ap()
    out_d = nc.dram_tensor('out', [D, NTOK], F32, kind='ExternalOutput').ap()

    es = ExitStack()
    sb = lambda name, shp, dt: es.enter_context(nc.sbuf_tensor(name, shp, dt))
    x_stage = sb('x_stage', [P, 8 * 1024], F32)
    xnb = sb('xnb', [P, 8 * 1024], BF16)
    x8a = sb('x8a', [P, 1024], FP8)
    x8b = sb('x8b', [P, 1024], FP8)
    xnfmA = [sb(f'xnfmA{e}', [P, NTOK], BF16) for e in range(8)]
    xnfmF = [sb(f'xnfmF{e}', [P, NTOK], BF16) for e in range(8)]
    wqkvb = sb('wqkvb', [P, 12 * 1024], BF16)
    wob = sb('wob', [P, 4 * 1024], BF16)
    wf8 = sb('wf8', [P, 8 * 1024], FP8)
    bo_s = sb('bo_s', [P, 8], F32)
    bfn_s = sb('bfn_s', [P, 8], F32)
    msk_s = sb('msk_s', [P, 512], F32)
    identB = sb('identB', [P, P], BF16)
    qpk = sb('qpk', [P, NTOK], BF16)
    kpk = sb('kpk', [P, NTOK], BF16)
    vfm = sb('vfm', [P, NTOK], BF16)
    epk = sb('epk', [P, NTOK], BF16)
    att = sb('att', [P, ATT_COLS], BF16)
    vtm = [sb(f'vtm{i}', [P, P], BF16) for i in range(4)]
    zsb = [sb(f'zsb{i}', [P, P], F32) for i in range(2)]
    nmx = sb('nmx', [P, 2], F32)
    smx = sb('smx', [P, 2], F32)
    rcx = sb('rcx', [P, 2], F32)
    st_s1 = sb('st_s1', [P, 8], F32)
    st_ss = sb('st_ss', [P, 8], F32)
    st_t1 = sb('st_t1', [P, 8], F32)
    st_t2 = sb('st_t2', [P, 8], F32)
    st_mean = sb('st_mean', [P, 8], F32)
    st_var = sb('st_var', [P, 8], F32)
    st_sd = sb('st_sd', [P, 8], F32)
    st_rs = sb('st_rs', [P, 8], F32)
    g16 = [sb(f'g16_{i}', [P, 512], FP16) for i in range(2)]
    gsl = [sb(f'gsl_{i}', [P, 512], F32) for i in range(2)]
    sg = [sb(f'sg_{i}', [P, 2048], F32) for i in range(2)]

    tpA = es.enter_context(nc.psum_tensor('tpA', [P, 2048], F32))
    tpB = es.enter_context(nc.psum_tensor('tpB', [P, 2048], F32))
    _tp = (tpA, tpB)
    pb = [(_tp[i // 4])[:, (i % 4) * 512:(i % 4 + 1) * 512] for i in range(8)]

    sem_h = {k: es.enter_context(nc.semaphore(f'sem_{k}'))
             for k in ('dA', 'dW', 'dC', 'dO', 'tS', 'vS', 'sS', 'gS',
                       'dC0', 'dC1', 'dC2', 'dC3', 'dO0', 'dO1', 'dO2', 'dO3')}
    S = {k: _Sem(h) for k, h in sem_h.items()}

    sched = {k: [] for k in ('tensor', 'vector', 'scalar', 'sync', 'gpsimd')}

    def E(eng, fn):
        sched[eng].append(fn)

    def W(eng, sem, val):
        if val > 0:
            E(eng, lambda e, s=S[sem].h, v=val: e.wait_ge(s, v))

    bank_state = [None] * 8
    rot = [0]

    def take_bank(eng):
        r = rot[0] % 8
        rot[0] += 1
        if bank_state[r] is not None:
            sem, val = bank_state[r]
            W(eng, sem, val)
        return r

    def bank_consumed(r, sem, val):
        bank_state[r] = (sem, val)

    def take_bank4(eng):
        if rot[0] % 4:
            rot[0] += 4 - rot[0] % 4
        r0 = rot[0] % 8
        rot[0] += 4
        for r in range(r0, r0 + 4):
            if bank_state[r] is not None:
                sem, val = bank_state[r]
                W(eng, sem, val)
        return r0

    v = S['gS'].inc(16)
    E('gpsimd', lambda g: g.dma_start(out=identB[:], in_=idn_d).then_inc(S['gS'].h, 16))
    ident_ok = ('gS', v)

    dw = {}
    for h in range(4):
        for m in range(3):
            v = S['dW'].inc(16)
            E('gpsimd', lambda g, h=h, m=m, v=v: g.dma_start(
                out=wqkvb[:, (h * 3 + m) * 1024:(h * 3 + m + 1) * 1024],
                in_=wqkv_d[h, m]).then_inc(S['dW'].h, 16))
    dw['wqkv'] = S['dW'].n
    for h in range(4):
        v = S['dW'].inc(16)
        E('gpsimd', lambda g, h=h: g.dma_start(
            out=wob[:, h * 1024:(h + 1) * 1024],
            in_=wo_d[h]).then_inc(S['dW'].h, 16))
    dw['wo'] = S['dW'].n
    for e in range(8):
        v = S['dW'].inc(16)
        E('gpsimd', lambda g, e=e: g.dma_start(
            out=wf8[:, e * 1024:(e + 1) * 1024],
            in_=wf_d[e]).then_inc(S['dW'].h, 16))
    dw['wf8'] = S['dW'].n
    S['dW'].inc(16)
    E('gpsimd', lambda g: g.dma_start(
        out=msk_s[:], in_=msk_d.transpose([1, 0, 2])).then_inc(S['dW'].h, 16))
    dw['msk'] = S['dW'].n
    S['dW'].inc(16)
    E('gpsimd', lambda g: g.dma_start(out=bo_s[:], in_=bo_d).then_inc(S['dW'].h, 16))
    S['dW'].inc(16)
    E('gpsimd', lambda g: g.dma_start(out=bfn_s[:], in_=bfn_d).then_inc(S['dW'].h, 16))
    dw['bias'] = S['dW'].n

    wave_src = [(xa_d, 0), (xa_d, 1), (xf_d, 0), (xf_d, 1)]
    xload = np.zeros((4, 8), np.int64)
    wave_consumed = [0]

    def emit_wave_load(w):
        src, half = wave_src[w]
        W('sync', 'vS', wave_consumed[0])
        for k in range(8):
            t = half * 8 + k
            xload[w, k] = S['dA'].inc(16)
            E('sync', lambda e, src=src, t=t, k=k: e.dma_start(
                out=x_stage[:, k * 1024:(k + 1) * 1024],
                in_=src[t * 128:(t + 1) * 128, :]).then_inc(S['dA'].h, 16))

    norm_done = np.zeros((4, 8), np.int64)
    xnb_ready = [[None] * NT, [None] * NT]
    x8_guard = [0, 0]
    xnb_free_t = [0]

    def emit_wave_stats(w):
        src, half = wave_src[w]
        fp8 = w >= 2
        setn = 1 if fp8 else 0
        sq_sv = 0
        W('vector', 'dA', int(xload[w, 7]))
        W('scalar', 'dA', int(xload[w, 7]))
        for k in range(8):
            E('vector', lambda e, k=k: e.reduce_sum(
                out=st_s1[:, k:k + 1],
                in_=x_stage[:, k * 1024:(k + 1) * 1024], axis=AX.X))
            E('vector', lambda e: e.nop(cycle_cnt=150))
            sq_sv = S['sS'].inc()
            E('scalar', lambda e, k=k: e.activation(
                out=epk[:, 0:1024], in_=x_stage[:, k * 1024:(k + 1) * 1024],
                func=AF.Square,
                accum_out=st_ss[:, k:k + 1]).then_inc(S['sS'].h, 1))
        W('vector', 'sS', sq_sv)
        E('vector', lambda e: e.tensor_scalar_mul(st_mean[:], st_s1[:], 1.0 / 1024.0))
        E('vector', lambda e: e.nop(cycle_cnt=150))
        E('vector', lambda e: e.tensor_scalar_mul(st_t1[:], st_ss[:], 1.0 / 1024.0))
        E('vector', lambda e: e.nop(cycle_cnt=150))
        E('vector', lambda e: e.tensor_mul(st_t2[:], st_mean[:], st_mean[:]))
        E('vector', lambda e: e.nop(cycle_cnt=150))
        E('vector', lambda e: e.tensor_sub(st_var[:], st_t1[:], st_t2[:]))
        E('vector', lambda e: e.nop(cycle_cnt=150))
        vv = S['vS'].inc()
        E('vector', lambda e: e.tensor_scalar_add(st_t1[:], st_var[:], EPS).then_inc(S['vS'].h, 1))
        W('scalar', 'vS', vv)
        sv = S['sS'].inc()
        E('scalar', lambda e: e.activation(
            out=st_sd[:], in_=st_t1[:], func=AF.Sqrt).then_inc(S['sS'].h, 1))
        W('vector', 'sS', sv)
        E('vector', lambda e: e.reciprocal(out=st_rs[:], in_=st_sd[:]))
        E('vector', lambda e: e.nop(cycle_cnt=150))
        for k in range(8):
            t = half * 8 + k
            if not fp8:
                W('vector', 'tS', xnb_free_t[0])
                vv = S['vS'].inc()
                E('vector', lambda e, k=k, t=t % 8: e.tensor_scalar(
                    out=xnb[:, t * 1024:(t + 1) * 1024],
                    in0=x_stage[:, k * 1024:(k + 1) * 1024],
                    scalar1=st_mean[:, k:k + 1], scalar2=st_rs[:, k:k + 1],
                    op0=ALU.subtract, op1=ALU.mult).then_inc(S['vS'].h, 1))
                norm_done[w, k] = vv
                xnb_ready[setn][t] = ('vS', vv)
            else:
                x8 = x8a if (t % 2 == 0) else x8b
                W('vector', 'sS', x8_guard[t % 2])
                vv = S['vS'].inc()
                E('vector', lambda e, k=k, x8=x8: e.tensor_scalar(
                    out=x8[:], in0=x_stage[:, k * 1024:(k + 1) * 1024],
                    scalar1=st_mean[:, k:k + 1], scalar2=st_rs[:, k:k + 1],
                    op0=ALU.subtract, op1=ALU.mult).then_inc(S['vS'].h, 1))
                norm_done[w, k] = vv
                W('scalar', 'vS', vv)
                W('scalar', 'tS', xnb_free_t[0])
                sv = S['sS'].inc()
                E('scalar', lambda e, t=t % 8, x8=x8: e.copy(
                    out=xnb[:, t * 1024:(t + 1) * 1024],
                    in_=x8[:]).then_inc(S['sS'].h, 1))
                x8_guard[t % 2] = sv
                xnb_ready[setn][t] = ('sS', sv)
        wave_consumed[0] = int(norm_done[w].max())

    fm_done = [[], []]
    _tp_first = [True]

    def emit_wave_transposes(w):
        setn = 0 if w < 2 else 1
        half = w % 2
        dst = xnfmA if setn == 0 else xnfmF
        for tg_l in range(2):
            tg = half * 2 + tg_l
            for ec in range(8):
                r0 = take_bank4('tensor')
                if _tp_first[0]:
                    W('tensor', ident_ok[0], ident_ok[1])
                    _tp_first[0] = False
                tv = S['tS'].inc()
                for k in range(4):
                    t = tg * 4 + k
                    sem, val = xnb_ready[setn][t]
                    W('tensor', sem, val)
                    bview = pb[r0 + k].bitcast(BF16)[:, 0:128]
                    E('tensor', lambda e, bview=bview, t=t % 8, ec=ec,
                      fin=(k == 3): _inc_if(e.transpose(
                        bview,
                        xnb[:, t * 1024 + ec * 128:t * 1024 + (ec + 1) * 128],
                        identB[:]), fin, S['tS'].h))
                eng = 'vector' if ((tg * 8 + ec) % 2 == 0) else 'scalar'
                W(eng, 'tS', tv)
                ev = S['vS' if eng == 'vector' else 'sS'].inc()
                for k in range(4):
                    bview = pb[r0 + k].bitcast(BF16)[:, 0:128]
                    dv = dst[ec][:, tg * 512 + k * 128:tg * 512 + (k + 1) * 128]
                    if eng == 'vector':
                        E(eng, lambda e, bview=bview, dv=dv, fin=(k == 3):
                          _inc_if(e.tensor_copy(out=dv, in_=bview), fin, S['vS'].h))
                    else:
                        E(eng, lambda e, bview=bview, dv=dv, fin=(k == 3):
                          _inc_if(e.copy(out=dv, in_=bview), fin, S['sS'].h))
                mark = ('vS' if eng == 'vector' else 'sS', ev)
                for r in range(r0, r0 + 4):
                    bank_consumed(r, *mark)
                fm_done[setn].append(mark)

    emit_wave_load(0)
    emit_wave_stats(0)
    emit_wave_transposes(0)
    xnb_free_t[0] = S['tS'].n
    emit_wave_load(1)
    emit_wave_stats(1)
    emit_wave_transposes(1)
    xnb_free_t[0] = S['tS'].n
    emit_wave_load(2)
    emit_wave_stats(2)
    emit_wave_transposes(2)
    xnb_free_t[0] = S['tS'].n
    emit_wave_load(3)
    emit_wave_stats(3)
    emit_wave_transposes(3)

    fmA_v = max([v for s_, v in fm_done[0] if s_ == 'vS'], default=0)
    fmA_s = max([v for s_, v in fm_done[0] if s_ == 'sS'], default=0)
    fmF_v = max([v for s_, v in fm_done[1] if s_ == 'vS'], default=0)
    fmF_s = max([v for s_, v in fm_done[1] if s_ == 'sS'], default=0)

    att_ev_last = 0
    qkv_done_t = 0
    first_qkv = True

    for h in range(4):
        dil = DILS[h]
        L = L_H[h]
        G = G_H[h]
        ncols = 32 * L
        packs = [(qpk, 1), (kpk, 0), (vfm, 2)]
        ev_marks = {'q': [], 'k': [], 'v': []}
        for name, (buf, m) in zip(('q', 'k', 'v'), packs):
            nblk = (ncols + 511) // 512
            for nb in range(nblk):
                n0 = nb * 512
                n1 = min(ncols, n0 + 512)
                N = n1 - n0
                s0 = n0 // L
                s1 = n1 // L
                r = take_bank('tensor')
                if first_qkv:
                    W('tensor', 'vS', fmA_v)
                    W('tensor', 'sS', fmA_s)
                    W('tensor', 'dW', dw['wqkv'])
                    first_qkv = False
                tv = S['tS'].inc()
                for ec in range(8):
                    rhs = xnfmA[ec][:].rearrange('p (s j) -> p s j', j=64)[
                        :, s0:s1, 0:64:dil]
                    E('tensor', lambda e, r=r, h=h, m=m, ec=ec, rhs=rhs, N=N:
                      _inc_if(e.matmul(pb[r][:, 0:N],
                               wqkvb[:, (h * 3 + m) * 1024 + ec * 128:
                                     (h * 3 + m) * 1024 + (ec + 1) * 128],
                               rhs, start=(ec == 0), stop=(ec == 7)),
                              ec == 7, S['tS'].h))
                qkv_done_t = tv
                eng = 'scalar' if name == 'k' else 'vector'
                W(eng, 'tS', tv)
                if eng == 'vector':
                    ev = S['vS'].inc()
                    E(eng, lambda e, r=r, buf=buf, n0=n0, N=N: e.tensor_copy(
                        out=buf[:, n0:n0 + N],
                        in_=pb[r][:, 0:N]).then_inc(S['vS'].h, 1))
                    mark = ('vS', ev)
                else:
                    ev = S['sS'].inc()
                    E(eng, lambda e, r=r, buf=buf, n0=n0, N=N: e.copy(
                        out=buf[:, n0:n0 + N],
                        in_=pb[r][:, 0:N]).then_inc(S['sS'].h, 1))
                    mark = ('sS', ev)
                bank_consumed(r, *mark)
                ev_marks[name].append(mark)
        qk_v = max([v for s_, v in ev_marks['q'] + ev_marks['v'] if s_ == 'vS'], default=0)
        k_s = max([v for s_, v in ev_marks['k'] if s_ == 'sS'], default=0)

        sc_t = [0] * G
        vt_t = [0] * G
        av_t = [0] * G
        sc_bank = [0] * G
        vt_bank = [0] * G
        av_bank = [0] * G
        exp_s = [0] * G
        vev_v = [0] * G

        def emit_scores(g):
            r = take_bank('tensor')
            sc_bank[g] = r
            if g == 0:
                W('tensor', 'vS', qk_v)
                W('tensor', 'sS', k_s)
            sc_t[g] = S['tS'].inc()
            E('tensor', lambda e, r=r, g=g: e.matmul(
                pb[r][:, 0:128], kpk[:, g * 128:(g + 1) * 128],
                qpk[:, g * 128:(g + 1) * 128], start=True,
                stop=True).then_inc(S['tS'].h, 1))

        def emit_vt(g):
            r = take_bank('tensor')
            vt_bank[g] = r
            E('tensor', lambda e, r=r, g=g: e.transpose(
                pb[r][:].bitcast(BF16)[:, 0:128],
                vfm[:, g * 128:(g + 1) * 128], identB[:]))
            vt_t[g] = S['tS'].inc()
            E('tensor', lambda e: e.nop().then_inc(S['tS'].h, 1))

        def emit_av(g):
            r = take_bank('tensor')
            av_bank[g] = r
            W('tensor', 'vS', vev_v[g])
            W('tensor', 'sS', exp_s[g])
            av_t[g] = S['tS'].inc()
            E('tensor', lambda e, r=r, g=g: e.matmul(
                pb[r][:, 0:128], vtm[g % 4][:],
                epk[:, g * 128:(g + 1) * 128], start=True,
                stop=True).then_inc(S['tS'].h, 1))

        def emit_softmax(g):
            W('vector', 'tS', sc_t[g])
            if h == 0 and g == 0:
                W('vector', 'dW', dw['msk'])
            if g >= 2:
                W('vector', 'sS', exp_s[g - 2])
            zb = zsb[g % 2]
            scb = sc_bank[g]
            E('vector', lambda e, zb=zb, h=h, scb=scb: e.scalar_tensor_tensor(
                out=zb[:], in0=pb[scb][:, 0:128], scalar=NORMC,
                in1=msk_s[:, h * 128:(h + 1) * 128], op0=ALU.mult, op1=ALU.add))
            vv = S['vS'].inc()
            E('vector', lambda e, g=g, zb=zb: e.reduce_max(
                out=nmx[:, g % 2:g % 2 + 1], in_=zb[:], axis=AX.X,
                negate=True).then_inc(S['vS'].h, 1))
            bank_consumed(sc_bank[g], 'vS', vv)
            W('scalar', 'vS', vv)
            sv = S['sS'].inc()
            E('scalar', lambda e, g=g, zb=zb: e.activation(
                out=epk[:, g * 128:(g + 1) * 128], in_=zb[:], func=AF.Exp,
                bias=nmx[:, g % 2:g % 2 + 1],
                scale=1.0).then_inc(S['sS'].h, 1))
            exp_s[g] = sv
            W('vector', 'sS', sv)
            E('vector', lambda e, g=g: e.reduce_sum(
                out=smx[:, g % 2:g % 2 + 1],
                in_=epk[:, g * 128:(g + 1) * 128], axis=AX.X))
            E('vector', lambda e: e.nop(cycle_cnt=150))
            E('vector', lambda e, g=g: e.reciprocal(
                out=rcx[:, g % 2:g % 2 + 1], in_=smx[:, g % 2:g % 2 + 1]))
            E('vector', lambda e: e.nop(cycle_cnt=150))
            W('vector', 'tS', vt_t[g])
            if g >= 4:
                W('vector', 'tS', av_t[g - 4])
            vv = S['vS'].inc()
            vtb = vt_bank[g]
            E('vector', lambda e, g=g, vtb=vtb: e.tensor_scalar_mul(
                vtm[g % 4][:],
                pb[vtb].bitcast(BF16)[:, 0:128],
                rcx[:, g % 2:g % 2 + 1]).then_inc(S['vS'].h, 1))
            bank_consumed(vt_bank[g], 'vS', vv)
            vev_v[g] = vv

        def emit_attev(g):
            W('scalar', 'tS', av_t[g])
            sv = S['sS'].inc()
            avb = av_bank[g]
            E('scalar', lambda e, g=g, h=h, avb=avb: e.copy(
                out=att[:, ATT_OFF[h] + g * 128:ATT_OFF[h] + (g + 1) * 128],
                in_=pb[avb][:, 0:128]).then_inc(S['sS'].h, 1))
            bank_consumed(avb, 'sS', sv)
            return sv

        for g in range(G):
            emit_scores(g)
            emit_vt(g)
            emit_softmax(g)
            if g >= 2:
                emit_av(g - 2)
                att_ev_last = emit_attev(g - 2)
        for g in range(max(0, G - 2), G):
            emit_av(g)
            att_ev_last = emit_attev(g)

    W('scalar', 'tS', qkv_done_t)
    W('scalar', 'dW', dw['wf8'])
    wfb_s = 0
    for e_ in range(8):
        sv = S['sS'].inc()
        E('scalar', lambda e, e_=e_: e.copy(
            out=wqkvb[:, e_ * 1024:(e_ + 1) * 1024],
            in_=wf8[:, e_ * 1024:(e_ + 1) * 1024]).then_inc(S['sS'].h, 1))
        wfb_s = sv

    def outproj_mms(dc, blk):
        r = []
        a1 = att[:, 0:2048]
        a2 = att[:, 2048:3072]
        a4 = att[:, 3072:3584]
        a8 = att[:, 3584:3840]
        if blk == 0:
            r.append((3, a8, True))
            r.append((2, a4.rearrange('p (s i c) -> p s i c', s=32, i=8, c=2)
                      .transpose([0, 3, 1, 2]), False))
            r.append((1, a2.rearrange('p (s i c r) -> p s i c r', s=32, i=8,
                                      c=2, r=2)[:, :, :, :, 0]
                      .transpose([0, 3, 1, 2]), False))
            r.append((0, a1.rearrange('p (s i c r) -> p s i c r', s=32, i=8,
                                      c=2, r=4)[:, :, :, :, 0]
                      .transpose([0, 3, 1, 2]), False))
        elif blk == 1:
            r.append((1, a2.rearrange('p (s i r) -> p s i r', s=32, i=16,
                                      r=2)[:, :, :, 1], True))
            r.append((0, a1.rearrange('p (s i r) -> p s i r', s=32, i=16,
                                      r=4)[:, :, :, 2], False))
        else:
            s0 = 0 if blk == 2 else 16
            r.append((0, a1.rearrange('p (s i r) -> p s i r', s=32, i=32,
                                      r=2)[:, s0:s0 + 16, :, 1], True))
        return r

    sg_guard = [0] * 4
    g16_guard = [0] * 2
    gsl_guard = [0] * 2
    first_tail = True
    store_v = 0

    for dc in range(8):
        for blk in range(4):
            it = dc * 4 + blk
            rg = take_bank('tensor')
            if first_tail:
                W('tensor', 'sS', max(wfb_s, fmF_s))
                W('tensor', 'vS', fmF_v)
            gmm = S['tS'].inc()
            for ec in range(8):
                E('tensor', lambda e, rg=rg, ec=ec, dc=dc, blk=blk: _inc_if(
                    e.matmul(
                        pb[rg][:, 0:512],
                        wqkvb[:, ec * 1024 + dc * 128:ec * 1024 + (dc + 1) * 128],
                        xnfmF[ec][:, blk * 512:(blk + 1) * 512],
                        start=(ec == 0), stop=(ec == 7)), ec == 7, S['tS'].h))
            rs = take_bank('tensor')
            if first_tail:
                W('tensor', 'sS', att_ev_last)
                W('tensor', 'dW', dw['wo'])
                first_tail = False
            mm_list = outproj_mms(dc, blk)
            smm = S['tS'].inc()
            for j, (hh, rhs, st_) in enumerate(mm_list):
                E('tensor', lambda e, rs=rs, hh=hh, rhs=rhs, dc=dc, st_=st_,
                  last=(j == len(mm_list) - 1):
                  _inc_if(e.matmul(
                      pb[rs][:, 0:rhs_n(rhs)],
                      wob[:, hh * 1024 + dc * 128:hh * 1024 + (dc + 1) * 128],
                      rhs, start=st_, stop=last), last, S['tS'].h))
            W('vector', 'tS', gmm)
            W('vector', 'sS', g16_guard[it % 2])
            vv = S['vS'].inc()
            E('vector', lambda e, rg=rg, it=it: e.tensor_copy(
                out=g16[it % 2][:], in_=pb[rg][:, 0:512]).then_inc(S['vS'].h, 1))
            bank_consumed(rg, 'vS', vv)
            W('scalar', 'vS', max(vv, gsl_guard[it % 2]))
            if it == 0:
                W('scalar', 'dW', dw['bias'])
            sv = S['sS'].inc()
            E('scalar', lambda e, it=it, dc=dc: e.activation(
                out=gsl[it % 2][:], in_=g16[it % 2][:], func=AF.Silu,
                bias=bfn_s[:, dc:dc + 1], scale=1.0).then_inc(S['sS'].h, 1))
            g16_guard[it % 2] = sv
            W('vector', 'tS', smm)
            W('vector', 'sS', sv)
            if it == 0:
                W('vector', 'dW', dw['bias'])
            if blk == 0 and sg_guard[dc % 2] != 0:
                W('vector', sg_guard[dc % 2][0], sg_guard[dc % 2][1])
            vv = S['vS'].inc()
            E('vector', lambda e, rs=rs, it=it, dc=dc, blk=blk:
              e.scalar_tensor_tensor(
                out=sg[dc % 2][:, blk * 512:(blk + 1) * 512],
                in0=pb[rs][:, 0:512],
                scalar=bo_s[:, dc:dc + 1], in1=gsl[it % 2][:],
                op0=ALU.add, op1=ALU.mult).then_inc(S['vS'].h, 1))
            bank_consumed(rs, 'vS', vv)
            gsl_guard[it % 2] = vv
            if blk == 3:
                W('gpsimd', 'vS', vv)
                cs = 'dC%d' % (dc % 2)
                cv = S[cs].inc(16)
                E('gpsimd', lambda e, dc=dc, cs=cs: e.dma_start(
                    out=sg[dc % 2][:],
                    in_=xffm_d[dc * 128:(dc + 1) * 128, :],
                    accum_op=ALU.add).then_inc(S[cs].h, 16))
                W('sync', cs, cv)
                os_ = 'dO%d' % (dc % 2)
                store_v = S[os_].inc(16)
                E('sync', lambda e, dc=dc, os_=os_: e.dma_start(
                    out=out_d[dc * 128:(dc + 1) * 128, :],
                    in_=sg[dc % 2][:]).then_inc(S[os_].h, 16))
                sg_guard[dc % 2] = (os_, store_v)

    for sl in range(2):
        W('sync', 'dO%d' % sl, S['dO%d' % sl].n)

    blk = es.enter_context(nc.Block())
    for eng_name, deco in (('sync', blk.sync), ('vector', blk.vector),
                           ('scalar', blk.scalar), ('tensor', blk.tensor),
                           ('gpsimd', blk.gpsimd)):
        def mk(eng_name):
            def body(e):
                for fn in sched[eng_name]:
                    fn(e)
            return body
        deco(mk(eng_name))
    es.close()
    return nc


def rhs_n(ap):
    n = 1
    for d in ap.shape[1:]:
        n *= d
    return n


_PERM = _pi_perm()


def kernel(L0_attn_x, L0_attn_Wk, L0_attn_Wq, L0_attn_Wv, L0_attn_W_out,
           L0_attn_b_out, L0_ffn_weight, L0_ffn_bias):
    x = np.ascontiguousarray(np.asarray(L0_attn_x, np.float32))
    Wk = np.asarray(L0_attn_Wk, np.float32)
    Wq = np.asarray(L0_attn_Wq, np.float32)
    Wv = np.asarray(L0_attn_Wv, np.float32)
    Wo = np.asarray(L0_attn_W_out, np.float32)
    bo = np.asarray(L0_attn_b_out, np.float32)
    Wf = np.asarray(L0_ffn_weight, np.float32)
    bf = np.asarray(L0_ffn_bias, np.float32)

    wqkv = np.empty((4, 3, P, 1024), np.float32)
    for h in range(4):
        for m, Wm in enumerate((Wk, Wq, Wv)):
            wqkv[h, m] = np.concatenate(
                [Wm[h][:, e * 128:(e + 1) * 128].T for e in range(8)], axis=1)
    wo = np.stack([Wo[:, h * 128:(h + 1) * 128].T for h in range(4)])
    wo = np.ascontiguousarray(wo)
    wf = np.stack([Wf[:, e * 128:(e + 1) * 128].T for e in range(8)])
    wf = np.ascontiguousarray(wf)
    bo_t = np.ascontiguousarray(bo.reshape(8, 128).T)
    bf_t = np.ascontiguousarray(bf.reshape(8, 128).T)
    msk = _masks()

    in_maps = []
    for c in range(8):
        b, q = c // 4, c % 4
        xa = np.ascontiguousarray(
            x[b].reshape(64, 128, 1024)[:, 32 * q:32 * (q + 1), :]
            .transpose(1, 0, 2).reshape(2048, 1024))
        xf_blkv = x[b, 2048 * q:2048 * (q + 1), :]
        xf = np.ascontiguousarray(xf_blkv[_PERM])
        xffm = np.ascontiguousarray(xf.T)
        in_maps.append(dict(xa=xa, xf=xf, xffm=xffm, wqkv=wqkv, wo=wo, wf=wf,
                            bo=bo_t, bfn=bf_t, msk=msk,
                            idn=np.eye(P, dtype=np.float32)))

    nc = _build()
    res = run_bass_kernel_spmd(nc, in_maps, core_ids=list(range(8)))

    y = np.empty((B, T, D), np.float32)
    for c in range(8):
        b, q = c // 4, c % 4
        y[b, 2048 * q + _PERM, :] = res.results[c]['out'].T
    return y


if __name__ == '__main__':
    rng = np.random.RandomState(0)
    ins = dict(
        L0_attn_x=rng.randn(B, T, D).astype(np.float32),
        L0_attn_Wk=rng.randn(4, 128, D).astype(np.float32) / np.sqrt(128),
        L0_attn_Wq=rng.randn(4, 128, D).astype(np.float32) / np.sqrt(128),
        L0_attn_Wv=rng.randn(4, 128, D).astype(np.float32) / np.sqrt(128),
        L0_attn_W_out=rng.randn(D, 512).astype(np.float32) / np.sqrt(512),
        L0_attn_b_out=np.zeros(D, np.float32),
        L0_ffn_weight=(rng.randn(D, D) * 0.02).astype(np.float32),
        L0_ffn_bias=np.zeros(D, np.float32),
    )
    out = kernel(**ins)
    print('kernel ran, out', out.shape, out.dtype, np.abs(out).mean())


# version 14
# speedup vs baseline: 1.0369x; 1.0369x over previous
import math
import sys
from contextlib import ExitStack

import numpy as np

try:
    import concourse.bass as bass
except ImportError:
    sys.path.insert(0, '/opt/trn_rl_repo')
    import concourse.bass as bass
import concourse.mybir as mybir
from concourse.bass_utils import run_bass_kernel_spmd

F32 = mybir.dt.float32
BF16 = mybir.dt.bfloat16
FP16 = mybir.dt.float16
FP8 = mybir.dt.float8e4
ALU = mybir.AluOpType
AF = mybir.ActivationFunctionType
AX = mybir.AxisListType

B, T, D = 2, 8192, 1024
SEG, NBLK = 128, 64
DILS = [1, 2, 4, 8]
NORMC = 1.0 / math.sqrt(128.0)
EPS = 1e-5
NT = 16
NTOK = 2048
P = 128
NEG = -1.0e30
L_H = [64, 32, 16, 8]
G_H = [16, 8, 4, 2]
ATT_OFF = [0, 2048, 3072, 3584]
ATT_COLS = 3840


def _pi_perm():
    offs = [np.arange(0, 64, 8), np.arange(4, 64, 8),
            np.arange(2, 64, 4), np.arange(1, 64, 2)]
    return np.concatenate(
        [(np.arange(32)[:, None] * 64 + o[None, :]).ravel() for o in offs])


def _masks():
    m = np.empty((4, P, P), np.float32)
    for h, L in enumerate(L_H):
        idx = np.arange(P)
        ph, n = idx // L, idx % L
        ok = (ph[:, None] == ph[None, :]) & (n[:, None] >= n[None, :])
        m[h] = np.where(ok, 0.0, NEG)
    return m


def _inc_if(inst, cond, sem):
    if cond:
        inst.then_inc(sem, 1)
    return inst


class _Sem:
    def __init__(self, handle):
        self.h = handle
        self.n = 0

    def inc(self, k=1):
        self.n += k
        return self.n


def _build():
    nc = bass.Bass('TRN2', target_bir_lowering=False)
    dt_in = dict(kind='ExternalInput')
    xa_d = nc.dram_tensor('xa', [NTOK, D], F32, **dt_in).ap()
    xf_d = nc.dram_tensor('xf', [NTOK, D], F32, **dt_in).ap()
    xffm_d = nc.dram_tensor('xffm', [D, NTOK], F32, **dt_in).ap()
    wqkv_d = nc.dram_tensor('wqkv', [4, 3, P, 1024], F32, **dt_in).ap()
    wo_d = nc.dram_tensor('wo', [4, P, 1024], F32, **dt_in).ap()
    wf_d = nc.dram_tensor('wf', [8, P, 1024], F32, **dt_in).ap()
    bo_d = nc.dram_tensor('bo', [P, 8], F32, **dt_in).ap()
    bfn_d = nc.dram_tensor('bfn', [P, 8], F32, **dt_in).ap()
    msk_d = nc.dram_tensor('msk', [4, P, P], F32, **dt_in).ap()
    idn_d = nc.dram_tensor('idn', [P, P], F32, **dt_in).ap()
    out_d = nc.dram_tensor('out', [D, NTOK], F32, kind='ExternalOutput').ap()

    es = ExitStack()
    sb = lambda name, shp, dt: es.enter_context(nc.sbuf_tensor(name, shp, dt))
    x_stage = sb('x_stage', [P, 8 * 1024], F32)
    xnb = sb('xnb', [P, 8 * 1024], BF16)
    x8a = sb('x8a', [P, 1024], FP8)
    x8b = sb('x8b', [P, 1024], FP8)
    xnfmA = [sb(f'xnfmA{e}', [P, NTOK], BF16) for e in range(8)]
    xnfmF = [sb(f'xnfmF{e}', [P, NTOK], BF16) for e in range(8)]
    wqkvb = sb('wqkvb', [P, 12 * 1024], BF16)
    wob = sb('wob', [P, 4 * 1024], BF16)
    wf8 = sb('wf8', [P, 8 * 1024], FP8)
    bo_s = sb('bo_s', [P, 8], F32)
    bfn_s = sb('bfn_s', [P, 8], F32)
    msk_s = sb('msk_s', [P, 512], F32)
    identB = sb('identB', [P, P], BF16)
    pad = sb('pad', [P, 512], F32)
    qpk = sb('qpk', [P, NTOK], BF16)
    kpk = sb('kpk', [P, NTOK], BF16)
    vfm = sb('vfm', [P, NTOK], BF16)
    epk = sb('epk', [P, NTOK], BF16)
    att = sb('att', [P, ATT_COLS], BF16)
    vtm = [sb(f'vtm{i}', [P, P], BF16) for i in range(4)]
    zsb = [sb(f'zsb{i}', [P, P], F32) for i in range(2)]
    nmx = sb('nmx', [P, 2], F32)
    smx = sb('smx', [P, 2], F32)
    rcx = sb('rcx', [P, 2], F32)
    st_s1 = sb('st_s1', [P, 8], F32)
    st_ss = sb('st_ss', [P, 8], F32)
    st_t1 = sb('st_t1', [P, 8], F32)
    st_t2 = sb('st_t2', [P, 8], F32)
    st_mean = sb('st_mean', [P, 8], F32)
    st_var = sb('st_var', [P, 8], F32)
    st_sd = sb('st_sd', [P, 8], F32)
    st_rs = sb('st_rs', [P, 8], F32)
    g16 = [sb(f'g16_{i}', [P, 512], FP16) for i in range(2)]
    gsl = [sb(f'gsl_{i}', [P, 512], F32) for i in range(2)]
    sg = [sb(f'sg_{i}', [P, 2048], F32) for i in range(2)]

    tpA = es.enter_context(nc.psum_tensor('tpA', [P, 2048], F32))
    tpB = es.enter_context(nc.psum_tensor('tpB', [P, 2048], F32))
    _tp = (tpA, tpB)
    pb = [(_tp[i // 4])[:, (i % 4) * 512:(i % 4 + 1) * 512] for i in range(8)]

    sem_h = {k: es.enter_context(nc.semaphore(f'sem_{k}'))
             for k in ('dA', 'dW', 'dC', 'dO', 'tS', 'vS', 'sS', 'gS',
                       'dC0', 'dC1', 'dC2', 'dC3', 'dO0', 'dO1', 'dO2', 'dO3')}
    S = {k: _Sem(h) for k, h in sem_h.items()}

    sched = {k: [] for k in ('tensor', 'vector', 'scalar', 'sync', 'gpsimd')}

    def E(eng, fn):
        sched[eng].append(fn)

    def W(eng, sem, val):
        if val > 0:
            E(eng, lambda e, s=S[sem].h, v=val: e.wait_ge(s, v))

    def VPAD(n):
        for _ in range(n):
            E('vector', lambda e: e.tensor_copy(out=pad[:, 0:128],
                                                in_=pad[:, 128:256]))

    def VPAD_INC(n, sem):
        for i in range(n):
            E('vector', lambda e, fin=(i == n - 1), sem=sem: _inc_if(
                e.tensor_copy(out=pad[:, 0:128], in_=pad[:, 128:256]),
                fin, sem))

    def SPAD_INC(n, sem):
        for i in range(n):
            E('scalar', lambda e, fin=(i == n - 1), sem=sem: _inc_if(
                e.copy(out=pad[:, 256:384], in_=pad[:, 384:512]),
                fin, sem))

    bank_state = [None] * 8
    rot = [0]

    def take_bank(eng):
        r = rot[0] % 8
        rot[0] += 1
        if bank_state[r] is not None:
            sem, val = bank_state[r]
            W(eng, sem, val)
        return r

    def bank_consumed(r, sem, val):
        bank_state[r] = (sem, val)

    def take_bank4(eng):
        if rot[0] % 4:
            rot[0] += 4 - rot[0] % 4
        r0 = rot[0] % 8
        rot[0] += 4
        for r in range(r0, r0 + 4):
            if bank_state[r] is not None:
                sem, val = bank_state[r]
                W(eng, sem, val)
        return r0

    v = S['gS'].inc(16)
    E('gpsimd', lambda g: g.dma_start(out=identB[:], in_=idn_d).then_inc(S['gS'].h, 16))
    ident_ok = ('gS', v)

    dw = {}
    for h in range(4):
        for m in range(3):
            v = S['dW'].inc(16)
            E('gpsimd', lambda g, h=h, m=m, v=v: g.dma_start(
                out=wqkvb[:, (h * 3 + m) * 1024:(h * 3 + m + 1) * 1024],
                in_=wqkv_d[h, m]).then_inc(S['dW'].h, 16))
    dw['wqkv'] = S['dW'].n
    for h in range(4):
        v = S['dW'].inc(16)
        E('gpsimd', lambda g, h=h: g.dma_start(
            out=wob[:, h * 1024:(h + 1) * 1024],
            in_=wo_d[h]).then_inc(S['dW'].h, 16))
    dw['wo'] = S['dW'].n
    for e in range(8):
        v = S['dW'].inc(16)
        E('gpsimd', lambda g, e=e: g.dma_start(
            out=wf8[:, e * 1024:(e + 1) * 1024],
            in_=wf_d[e]).then_inc(S['dW'].h, 16))
    dw['wf8'] = S['dW'].n
    S['dW'].inc(16)
    E('gpsimd', lambda g: g.dma_start(
        out=msk_s[:], in_=msk_d.transpose([1, 0, 2])).then_inc(S['dW'].h, 16))
    dw['msk'] = S['dW'].n
    S['dW'].inc(16)
    E('gpsimd', lambda g: g.dma_start(out=bo_s[:], in_=bo_d).then_inc(S['dW'].h, 16))
    S['dW'].inc(16)
    E('gpsimd', lambda g: g.dma_start(out=bfn_s[:], in_=bfn_d).then_inc(S['dW'].h, 16))
    dw['bias'] = S['dW'].n

    wave_src = [(xa_d, 0), (xa_d, 1), (xf_d, 0), (xf_d, 1)]
    xload = np.zeros((4, 8), np.int64)
    wave_consumed = [0]

    def emit_wave_load(w):
        src, half = wave_src[w]
        W('sync', 'vS', wave_consumed[0])
        for k in range(8):
            t = half * 8 + k
            xload[w, k] = S['dA'].inc(16)
            E('sync', lambda e, src=src, t=t, k=k: e.dma_start(
                out=x_stage[:, k * 1024:(k + 1) * 1024],
                in_=src[t * 128:(t + 1) * 128, :]).then_inc(S['dA'].h, 16))

    norm_done = np.zeros((4, 8), np.int64)
    xnb_ready = [[None] * NT, [None] * NT]
    x8_guard = [0, 0]
    xnb_free_t = [0]

    def emit_wave_stats(w):
        src, half = wave_src[w]
        fp8 = w >= 2
        setn = 1 if fp8 else 0
        sq_sv = 0
        W('vector', 'dA', int(xload[w, 7]))
        W('scalar', 'dA', int(xload[w, 7]))
        for k in range(8):
            E('vector', lambda e, k=k: e.reduce_sum(
                out=st_s1[:, k:k + 1],
                in_=x_stage[:, k * 1024:(k + 1) * 1024], axis=AX.X))
            VPAD(1)
            sq_sv = S['sS'].inc()
            E('scalar', lambda e, k=k: e.activation(
                out=epk[:, 0:1024], in_=x_stage[:, k * 1024:(k + 1) * 1024],
                func=AF.Square,
                accum_out=st_ss[:, k:k + 1]).then_inc(S['sS'].h, 1))
        W('vector', 'sS', sq_sv)
        E('vector', lambda e: e.tensor_scalar_mul(st_mean[:], st_s1[:], 1.0 / 1024.0))
        VPAD(1)
        E('vector', lambda e: e.tensor_scalar_mul(st_t1[:], st_ss[:], 1.0 / 1024.0))
        VPAD(1)
        E('vector', lambda e: e.tensor_mul(st_t2[:], st_mean[:], st_mean[:]))
        VPAD(1)
        E('vector', lambda e: e.tensor_sub(st_var[:], st_t1[:], st_t2[:]))
        VPAD(1)
        vv = S['vS'].inc()
        E('vector', lambda e: e.tensor_scalar_add(st_t1[:], st_var[:], EPS).then_inc(S['vS'].h, 1))
        W('scalar', 'vS', vv)
        sv = S['sS'].inc()
        E('scalar', lambda e: e.activation(
            out=st_sd[:], in_=st_t1[:], func=AF.Sqrt).then_inc(S['sS'].h, 1))
        W('vector', 'sS', sv)
        E('vector', lambda e: e.reciprocal(out=st_rs[:], in_=st_sd[:]))
        VPAD(1)
        for k in range(8):
            t = half * 8 + k
            if not fp8:
                W('vector', 'tS', xnb_free_t[0])
                vv = S['vS'].inc()
                E('vector', lambda e, k=k, t=t % 8: e.tensor_scalar(
                    out=xnb[:, t * 1024:(t + 1) * 1024],
                    in0=x_stage[:, k * 1024:(k + 1) * 1024],
                    scalar1=st_mean[:, k:k + 1], scalar2=st_rs[:, k:k + 1],
                    op0=ALU.subtract, op1=ALU.mult).then_inc(S['vS'].h, 1))
                norm_done[w, k] = vv
                xnb_ready[setn][t] = ('vS', vv)
            else:
                x8 = x8a if (t % 2 == 0) else x8b
                W('vector', 'sS', x8_guard[t % 2])
                vv = S['vS'].inc()
                E('vector', lambda e, k=k, x8=x8: e.tensor_scalar(
                    out=x8[:], in0=x_stage[:, k * 1024:(k + 1) * 1024],
                    scalar1=st_mean[:, k:k + 1], scalar2=st_rs[:, k:k + 1],
                    op0=ALU.subtract, op1=ALU.mult).then_inc(S['vS'].h, 1))
                norm_done[w, k] = vv
                W('scalar', 'vS', vv)
                W('scalar', 'tS', xnb_free_t[0])
                sv = S['sS'].inc()
                E('scalar', lambda e, t=t % 8, x8=x8: e.copy(
                    out=xnb[:, t * 1024:(t + 1) * 1024],
                    in_=x8[:]).then_inc(S['sS'].h, 1))
                x8_guard[t % 2] = sv
                xnb_ready[setn][t] = ('sS', sv)
        wave_consumed[0] = int(norm_done[w].max())

    fm_done = [[], []]
    _tp_first = [True]

    def emit_wave_transposes(w):
        setn = 0 if w < 2 else 1
        half = w % 2
        dst = xnfmA if setn == 0 else xnfmF
        for tg_l in range(2):
            tg = half * 2 + tg_l
            for ec in range(8):
                r0 = take_bank4('tensor')
                if _tp_first[0]:
                    W('tensor', ident_ok[0], ident_ok[1])
                    _tp_first[0] = False
                tv = S['tS'].inc()
                for k in range(4):
                    t = tg * 4 + k
                    sem, val = xnb_ready[setn][t]
                    W('tensor', sem, val)
                    bview = pb[r0 + k].bitcast(BF16)[:, 0:128]
                    E('tensor', lambda e, bview=bview, t=t % 8, ec=ec,
                      fin=(k == 3): _inc_if(e.transpose(
                        bview,
                        xnb[:, t * 1024 + ec * 128:t * 1024 + (ec + 1) * 128],
                        identB[:]), fin, S['tS'].h))
                eng = 'vector' if ((tg * 8 + ec) % 2 == 0) else 'scalar'
                W(eng, 'tS', tv)
                ev = S['vS' if eng == 'vector' else 'sS'].inc()
                for k in range(4):
                    bview = pb[r0 + k].bitcast(BF16)[:, 0:128]
                    dv = dst[ec][:, tg * 512 + k * 128:tg * 512 + (k + 1) * 128]
                    if eng == 'vector':
                        E(eng, lambda e, bview=bview, dv=dv, fin=(k == 3):
                          _inc_if(e.tensor_copy(out=dv, in_=bview), fin, S['vS'].h))
                    else:
                        E(eng, lambda e, bview=bview, dv=dv, fin=(k == 3):
                          _inc_if(e.copy(out=dv, in_=bview), fin, S['sS'].h))
                mark = ('vS' if eng == 'vector' else 'sS', ev)
                for r in range(r0, r0 + 4):
                    bank_consumed(r, *mark)
                fm_done[setn].append(mark)

    emit_wave_load(0)
    emit_wave_stats(0)
    emit_wave_transposes(0)
    xnb_free_t[0] = S['tS'].n
    emit_wave_load(1)
    emit_wave_stats(1)
    emit_wave_transposes(1)
    xnb_free_t[0] = S['tS'].n
    emit_wave_load(2)
    emit_wave_stats(2)
    emit_wave_transposes(2)
    xnb_free_t[0] = S['tS'].n
    emit_wave_load(3)
    emit_wave_stats(3)
    emit_wave_transposes(3)

    fmA_v = max([v for s_, v in fm_done[0] if s_ == 'vS'], default=0)
    fmA_s = max([v for s_, v in fm_done[0] if s_ == 'sS'], default=0)
    fmF_v = max([v for s_, v in fm_done[1] if s_ == 'vS'], default=0)
    fmF_s = max([v for s_, v in fm_done[1] if s_ == 'sS'], default=0)

    att_ev_last = 0
    qkv_done_t = 0
    first_qkv = True

    for h in range(4):
        dil = DILS[h]
        L = L_H[h]
        G = G_H[h]
        ncols = 32 * L
        packs = [(qpk, 1), (kpk, 0), (vfm, 2)]
        ev_marks = {'q': [], 'k': [], 'v': []}
        for name, (buf, m) in zip(('q', 'k', 'v'), packs):
            nblk = (ncols + 511) // 512
            for nb in range(nblk):
                n0 = nb * 512
                n1 = min(ncols, n0 + 512)
                N = n1 - n0
                s0 = n0 // L
                s1 = n1 // L
                r = take_bank('tensor')
                if first_qkv:
                    W('tensor', 'vS', fmA_v)
                    W('tensor', 'sS', fmA_s)
                    W('tensor', 'dW', dw['wqkv'])
                    first_qkv = False
                tv = S['tS'].inc()
                for ec in range(8):
                    rhs = xnfmA[ec][:].rearrange('p (s j) -> p s j', j=64)[
                        :, s0:s1, 0:64:dil]
                    E('tensor', lambda e, r=r, h=h, m=m, ec=ec, rhs=rhs, N=N:
                      _inc_if(e.matmul(pb[r][:, 0:N],
                               wqkvb[:, (h * 3 + m) * 1024 + ec * 128:
                                     (h * 3 + m) * 1024 + (ec + 1) * 128],
                               rhs, start=(ec == 0), stop=(ec == 7)),
                              ec == 7, S['tS'].h))
                qkv_done_t = tv
                eng = 'scalar' if name == 'k' else 'vector'
                W(eng, 'tS', tv)
                if eng == 'vector':
                    ev = S['vS'].inc()
                    E(eng, lambda e, r=r, buf=buf, n0=n0, N=N: e.tensor_copy(
                        out=buf[:, n0:n0 + N],
                        in_=pb[r][:, 0:N]).then_inc(S['vS'].h, 1))
                    mark = ('vS', ev)
                else:
                    ev = S['sS'].inc()
                    E(eng, lambda e, r=r, buf=buf, n0=n0, N=N: e.copy(
                        out=buf[:, n0:n0 + N],
                        in_=pb[r][:, 0:N]).then_inc(S['sS'].h, 1))
                    mark = ('sS', ev)
                bank_consumed(r, *mark)
                ev_marks[name].append(mark)
        qk_v = max([v for s_, v in ev_marks['q'] + ev_marks['v'] if s_ == 'vS'], default=0)
        k_s = max([v for s_, v in ev_marks['k'] if s_ == 'sS'], default=0)

        sc_t = [0] * G
        vt_t = [0] * G
        av_t = [0] * G
        sc_bank = [0] * G
        vt_bank = [0] * G
        av_bank = [0] * G
        exp_s = [0] * G
        vev_v = [0] * G

        def emit_scores(g):
            r = take_bank('tensor')
            sc_bank[g] = r
            if g == 0:
                W('tensor', 'vS', qk_v)
                W('tensor', 'sS', k_s)
            sc_t[g] = S['tS'].inc()
            E('tensor', lambda e, r=r, g=g: e.matmul(
                pb[r][:, 0:128], kpk[:, g * 128:(g + 1) * 128],
                qpk[:, g * 128:(g + 1) * 128], start=True,
                stop=True).then_inc(S['tS'].h, 1))

        def emit_vt(g):
            r = take_bank('tensor')
            vt_bank[g] = r
            E('tensor', lambda e, r=r, g=g: e.transpose(
                pb[r][:].bitcast(BF16)[:, 0:128],
                vfm[:, g * 128:(g + 1) * 128], identB[:]))
            vt_t[g] = S['tS'].inc()
            E('tensor', lambda e: e.nop().then_inc(S['tS'].h, 1))

        def emit_av(g):
            r = take_bank('tensor')
            av_bank[g] = r
            W('tensor', 'vS', vev_v[g])
            W('tensor', 'sS', exp_s[g])
            av_t[g] = S['tS'].inc()
            E('tensor', lambda e, r=r, g=g: e.matmul(
                pb[r][:, 0:128], vtm[g % 4][:],
                epk[:, g * 128:(g + 1) * 128], start=True,
                stop=True).then_inc(S['tS'].h, 1))

        def emit_softmax(g):
            W('vector', 'tS', sc_t[g])
            if h == 0 and g == 0:
                W('vector', 'dW', dw['msk'])
            if g >= 2:
                W('vector', 'sS', exp_s[g - 2])
            zb = zsb[g % 2]
            scb = sc_bank[g]
            E('vector', lambda e, zb=zb, h=h, scb=scb: e.scalar_tensor_tensor(
                out=zb[:], in0=pb[scb][:, 0:128], scalar=NORMC,
                in1=msk_s[:, h * 128:(h + 1) * 128], op0=ALU.mult, op1=ALU.add))
            vv = S['vS'].inc()
            E('vector', lambda e, g=g, zb=zb: e.reduce_max(
                out=nmx[:, g % 2:g % 2 + 1], in_=zb[:], axis=AX.X,
                negate=True).then_inc(S['vS'].h, 1))
            bank_consumed(sc_bank[g], 'vS', vv)
            W('scalar', 'vS', vv)
            sv = S['sS'].inc()
            E('scalar', lambda e, g=g, zb=zb: e.activation(
                out=epk[:, g * 128:(g + 1) * 128], in_=zb[:], func=AF.Exp,
                bias=nmx[:, g % 2:g % 2 + 1],
                scale=1.0).then_inc(S['sS'].h, 1))
            exp_s[g] = sv
            W('vector', 'sS', sv)
            E('vector', lambda e, g=g: e.reduce_sum(
                out=smx[:, g % 2:g % 2 + 1],
                in_=epk[:, g * 128:(g + 1) * 128], axis=AX.X))
            VPAD(1)
            E('vector', lambda e, g=g: e.reciprocal(
                out=rcx[:, g % 2:g % 2 + 1], in_=smx[:, g % 2:g % 2 + 1]))
            VPAD(1)
            W('vector', 'tS', vt_t[g])
            if g >= 4:
                W('vector', 'tS', av_t[g - 4])
            vv = S['vS'].inc()
            vtb = vt_bank[g]
            E('vector', lambda e, g=g, vtb=vtb: e.tensor_scalar_mul(
                vtm[g % 4][:],
                pb[vtb].bitcast(BF16)[:, 0:128],
                rcx[:, g % 2:g % 2 + 1]).then_inc(S['vS'].h, 1))
            bank_consumed(vt_bank[g], 'vS', vv)
            vev_v[g] = vv

        def emit_attev(g):
            W('scalar', 'tS', av_t[g])
            sv = S['sS'].inc()
            avb = av_bank[g]
            E('scalar', lambda e, g=g, h=h, avb=avb: e.copy(
                out=att[:, ATT_OFF[h] + g * 128:ATT_OFF[h] + (g + 1) * 128],
                in_=pb[avb][:, 0:128]).then_inc(S['sS'].h, 1))
            bank_consumed(avb, 'sS', sv)
            return sv

        for g in range(G):
            emit_scores(g)
            emit_vt(g)
            emit_softmax(g)
            if g >= 2:
                emit_av(g - 2)
                att_ev_last = emit_attev(g - 2)
        for g in range(max(0, G - 2), G):
            emit_av(g)
            att_ev_last = emit_attev(g)

    W('scalar', 'tS', qkv_done_t)
    W('scalar', 'dW', dw['wf8'])
    wfb_s = 0
    for e_ in range(8):
        sv = S['sS'].inc()
        E('scalar', lambda e, e_=e_: e.copy(
            out=wqkvb[:, e_ * 1024:(e_ + 1) * 1024],
            in_=wf8[:, e_ * 1024:(e_ + 1) * 1024]).then_inc(S['sS'].h, 1))
        wfb_s = sv

    def outproj_mms(dc, blk):
        r = []
        a1 = att[:, 0:2048]
        a2 = att[:, 2048:3072]
        a4 = att[:, 3072:3584]
        a8 = att[:, 3584:3840]
        if blk == 0:
            r.append((3, a8, True))
            r.append((2, a4.rearrange('p (s i c) -> p s i c', s=32, i=8, c=2)
                      .transpose([0, 3, 1, 2]), False))
            r.append((1, a2.rearrange('p (s i c r) -> p s i c r', s=32, i=8,
                                      c=2, r=2)[:, :, :, :, 0]
                      .transpose([0, 3, 1, 2]), False))
            r.append((0, a1.rearrange('p (s i c r) -> p s i c r', s=32, i=8,
                                      c=2, r=4)[:, :, :, :, 0]
                      .transpose([0, 3, 1, 2]), False))
        elif blk == 1:
            r.append((1, a2.rearrange('p (s i r) -> p s i r', s=32, i=16,
                                      r=2)[:, :, :, 1], True))
            r.append((0, a1.rearrange('p (s i r) -> p s i r', s=32, i=16,
                                      r=4)[:, :, :, 2], False))
        else:
            s0 = 0 if blk == 2 else 16
            r.append((0, a1.rearrange('p (s i r) -> p s i r', s=32, i=32,
                                      r=2)[:, s0:s0 + 16, :, 1], True))
        return r

    sg_guard = [0] * 4
    g16_guard = [0] * 2
    gsl_guard = [0] * 2
    first_tail = True
    store_v = 0

    for dc in range(8):
        for blk in range(4):
            it = dc * 4 + blk
            rg = take_bank('tensor')
            if first_tail:
                W('tensor', 'sS', max(wfb_s, fmF_s))
                W('tensor', 'vS', fmF_v)
            gmm = S['tS'].inc()
            for ec in range(8):
                E('tensor', lambda e, rg=rg, ec=ec, dc=dc, blk=blk: _inc_if(
                    e.matmul(
                        pb[rg][:, 0:512],
                        wqkvb[:, ec * 1024 + dc * 128:ec * 1024 + (dc + 1) * 128],
                        xnfmF[ec][:, blk * 512:(blk + 1) * 512],
                        start=(ec == 0), stop=(ec == 7)), ec == 7, S['tS'].h))
            rs = take_bank('tensor')
            if first_tail:
                W('tensor', 'sS', att_ev_last)
                W('tensor', 'dW', dw['wo'])
                first_tail = False
            mm_list = outproj_mms(dc, blk)
            smm = S['tS'].inc()
            for j, (hh, rhs, st_) in enumerate(mm_list):
                E('tensor', lambda e, rs=rs, hh=hh, rhs=rhs, dc=dc, st_=st_,
                  last=(j == len(mm_list) - 1):
                  _inc_if(e.matmul(
                      pb[rs][:, 0:rhs_n(rhs)],
                      wob[:, hh * 1024 + dc * 128:hh * 1024 + (dc + 1) * 128],
                      rhs, start=st_, stop=last), last, S['tS'].h))
            W('vector', 'tS', gmm)
            W('vector', 'sS', g16_guard[it % 2])
            vv = S['vS'].inc()
            E('vector', lambda e, rg=rg, it=it: e.tensor_copy(
                out=g16[it % 2][:], in_=pb[rg][:, 0:512]).then_inc(S['vS'].h, 1))
            bank_consumed(rg, 'vS', vv)
            W('scalar', 'vS', max(vv, gsl_guard[it % 2]))
            if it == 0:
                W('scalar', 'dW', dw['bias'])
            sv = S['sS'].inc()
            E('scalar', lambda e, it=it, dc=dc: e.activation(
                out=gsl[it % 2][:], in_=g16[it % 2][:], func=AF.Silu,
                bias=bfn_s[:, dc:dc + 1], scale=1.0).then_inc(S['sS'].h, 1))
            g16_guard[it % 2] = sv
            W('vector', 'tS', smm)
            W('vector', 'sS', sv)
            if it == 0:
                W('vector', 'dW', dw['bias'])
            if blk == 0 and sg_guard[dc % 2] != 0:
                W('vector', sg_guard[dc % 2][0], sg_guard[dc % 2][1])
            vv = S['vS'].inc()
            E('vector', lambda e, rs=rs, it=it, dc=dc, blk=blk:
              e.scalar_tensor_tensor(
                out=sg[dc % 2][:, blk * 512:(blk + 1) * 512],
                in0=pb[rs][:, 0:512],
                scalar=bo_s[:, dc:dc + 1], in1=gsl[it % 2][:],
                op0=ALU.add, op1=ALU.mult).then_inc(S['vS'].h, 1))
            bank_consumed(rs, 'vS', vv)
            gsl_guard[it % 2] = vv
            if blk == 3:
                W('gpsimd', 'vS', vv)
                cs = 'dC%d' % (dc % 2)
                cv = S[cs].inc(16)
                E('gpsimd', lambda e, dc=dc, cs=cs: e.dma_start(
                    out=sg[dc % 2][:],
                    in_=xffm_d[dc * 128:(dc + 1) * 128, :],
                    accum_op=ALU.add).then_inc(S[cs].h, 16))
                W('sync', cs, cv)
                os_ = 'dO%d' % (dc % 2)
                store_v = S[os_].inc(16)
                E('sync', lambda e, dc=dc, os_=os_: e.dma_start(
                    out=out_d[dc * 128:(dc + 1) * 128, :],
                    in_=sg[dc % 2][:]).then_inc(S[os_].h, 16))
                sg_guard[dc % 2] = (os_, store_v)

    for sl in range(2):
        W('sync', 'dO%d' % sl, S['dO%d' % sl].n)

    blk = es.enter_context(nc.Block())
    for eng_name, deco in (('sync', blk.sync), ('vector', blk.vector),
                           ('scalar', blk.scalar), ('tensor', blk.tensor),
                           ('gpsimd', blk.gpsimd)):
        def mk(eng_name):
            def body(e):
                for fn in sched[eng_name]:
                    fn(e)
            return body
        deco(mk(eng_name))
    es.close()
    return nc


def rhs_n(ap):
    n = 1
    for d in ap.shape[1:]:
        n *= d
    return n


_PERM = _pi_perm()


def kernel(L0_attn_x, L0_attn_Wk, L0_attn_Wq, L0_attn_Wv, L0_attn_W_out,
           L0_attn_b_out, L0_ffn_weight, L0_ffn_bias):
    x = np.ascontiguousarray(np.asarray(L0_attn_x, np.float32))
    Wk = np.asarray(L0_attn_Wk, np.float32)
    Wq = np.asarray(L0_attn_Wq, np.float32)
    Wv = np.asarray(L0_attn_Wv, np.float32)
    Wo = np.asarray(L0_attn_W_out, np.float32)
    bo = np.asarray(L0_attn_b_out, np.float32)
    Wf = np.asarray(L0_ffn_weight, np.float32)
    bf = np.asarray(L0_ffn_bias, np.float32)

    wqkv = np.empty((4, 3, P, 1024), np.float32)
    for h in range(4):
        for m, Wm in enumerate((Wk, Wq, Wv)):
            wqkv[h, m] = np.concatenate(
                [Wm[h][:, e * 128:(e + 1) * 128].T for e in range(8)], axis=1)
    wo = np.stack([Wo[:, h * 128:(h + 1) * 128].T for h in range(4)])
    wo = np.ascontiguousarray(wo)
    wf = np.stack([Wf[:, e * 128:(e + 1) * 128].T for e in range(8)])
    wf = np.ascontiguousarray(wf)
    bo_t = np.ascontiguousarray(bo.reshape(8, 128).T)
    bf_t = np.ascontiguousarray(bf.reshape(8, 128).T)
    msk = _masks()

    in_maps = []
    for c in range(8):
        b, q = c // 4, c % 4
        xa = np.ascontiguousarray(
            x[b].reshape(64, 128, 1024)[:, 32 * q:32 * (q + 1), :]
            .transpose(1, 0, 2).reshape(2048, 1024))
        xf_blkv = x[b, 2048 * q:2048 * (q + 1), :]
        xf = np.ascontiguousarray(xf_blkv[_PERM])
        xffm = np.ascontiguousarray(xf.T)
        in_maps.append(dict(xa=xa, xf=xf, xffm=xffm, wqkv=wqkv, wo=wo, wf=wf,
                            bo=bo_t, bfn=bf_t, msk=msk,
                            idn=np.eye(P, dtype=np.float32)))

    nc = _build()
    res = run_bass_kernel_spmd(nc, in_maps, core_ids=list(range(8)))

    y = np.empty((B, T, D), np.float32)
    for c in range(8):
        b, q = c // 4, c % 4
        y[b, 2048 * q + _PERM, :] = res.results[c]['out'].T
    return y


if __name__ == '__main__':
    rng = np.random.RandomState(0)
    ins = dict(
        L0_attn_x=rng.randn(B, T, D).astype(np.float32),
        L0_attn_Wk=rng.randn(4, 128, D).astype(np.float32) / np.sqrt(128),
        L0_attn_Wq=rng.randn(4, 128, D).astype(np.float32) / np.sqrt(128),
        L0_attn_Wv=rng.randn(4, 128, D).astype(np.float32) / np.sqrt(128),
        L0_attn_W_out=rng.randn(D, 512).astype(np.float32) / np.sqrt(512),
        L0_attn_b_out=np.zeros(D, np.float32),
        L0_ffn_weight=(rng.randn(D, D) * 0.02).astype(np.float32),
        L0_ffn_bias=np.zeros(D, np.float32),
    )
    out = kernel(**ins)
    print('kernel ran, out', out.shape, out.dtype, np.abs(out).mean())
